# version 09 (final)
# baseline (speedup 1.0000x reference)
import numpy as np
import concourse.bass as bass
import concourse.mybir as mybir
from concourse.bass_utils import run_bass_kernel_spmd

F32 = mybir.dt.float32
BF16 = mybir.dt.bfloat16
I32 = mybir.dt.int32
ALU = mybir.AluOpType
ACT = mybir.ActivationFunctionType

D = 2048
T = 2048
Q = 1024
NH = 16
HD = 128
HID = 5632
DC = 16
HC = 44
EPS_LN = 1e-5
EPS_Q = float(np.float32(1e-8))
N_FFN_W = float(HID * D)

_PI = np.pi
C1 = float(np.float32(_PI))
C2 = float(np.float32(_PI - np.float64(np.float32(_PI))))
C3 = float(np.float32(_PI - np.float64(np.float32(_PI)) - np.float64(np.float32(_PI - np.float64(np.float32(_PI))))))
P1 = float(np.float32(_PI / 2))
P2 = float(np.float32(_PI / 2 - np.float64(np.float32(_PI / 2))))
INVPI = float(np.float32(1.0 / _PI))
SQRT_HD = float(np.float32(np.sqrt(np.float32(128.0))))

ENGINES = ("sync", "gpsimd", "tensor", "scalar", "vector")
KB = 1024


class Plan:
    def __init__(self):
        self.dummy_dma = None
        self.stream = {e: [] for e in ENGINES}
        self.count = {}
        self.lastwait = {}
        self.nbar = 0

    def n(self, sem):
        return self.count.get(sem, 0)

    def emit(self, eng, fn, waits=(), inc=None):
        st = self.stream[eng]
        for (s, v) in waits:
            if v <= 0:
                continue
            key = (eng, s)
            if v > self.lastwait.get(key, 0):
                self.lastwait[key] = v
                st.append(("w", s, v))
        newinc = None
        if inc is not None:
            s, k = inc
            self.count[s] = self.count.get(s, 0) + k
            newinc = (s, k)
        if fn is not None:
            st.append(("x", fn, newinc))

    def wait(self, eng, sem, val):
        self.emit(eng, None, waits=((sem, val),))

    def barrier(self):
        if self.dummy_dma is not None:
            self.dummy_dma()
        self.wait("sync", "ds", 16 * self.n("dsn"))
        self.nbar += 1
        target = self.nbar * len(ENGINES)
        for e in ENGINES:
            self.emit(e, lambda eng: eng.drain(), inc=("bar", 1))
        for e in ENGINES:
            self.wait(e, "bar", target)
        for e in ENGINES:
            for s in list(self.count):
                if s in ("dsn", "outn", "out"):
                    continue
                v = 16 * self.n("dsn") if s == "ds" else self.count[s]
                self.lastwait[(e, s)] = max(self.lastwait.get((e, s), 0), v)


def build_program(trunc=0, phmask=None):
    def ph_on(tag):
        return phmask is None or tag in phmask
    nc = bass.Bass("TRN2", target_bir_lowering=False, detect_race_conditions=False)

    def din(name, shape, dt=F32):
        return nc.dram_tensor(name, shape, dt, kind="ExternalInput").ap()

    XF = din("XF", [D, T])
    XO = din("XO", [D, Q])
    POS = din("POS", [128, T])
    POSQ = din("POSQ", [128, Q])
    IROW = din("IROW", [128, 1])
    SGNC = din("SGNC", [128, 1])
    PSWP = din("PSWP", [128, 128])
    WQ = din("WQ", [D, D])
    WK = din("WK", [D, HD])
    WV = din("WV", [D, HD])
    WO = din("WO", [D, D])
    BQ = din("BQ", [128, NH])
    BO = din("BO", [128, NH])
    BK = din("BK", [128, 1])
    BVR = din("BVR", [1, 128])
    W1T = din("W1T", [D, HID])
    W2T = din("W2T", [D, HID])
    W3T = din("W3T", [HID, D])
    Y = nc.dram_tensor("Y", [D, Q], F32, kind="ExternalOutput").ap()

    import contextlib
    ctx = contextlib.ExitStack()
    with ctx:
        ARENA_BYTES = 186 * KB
        backing = ctx.enter_context(
            nc.sbuf_tensor("arena", [128, ARENA_BYTES], mybir.dt.uint8))
        base = nc.lookup_mloc(backing).addr

        def place(name, shape, dtype, off):
            per = int(np.prod(shape[1:])) * mybir.dt.size(dtype)
            assert off + per <= ARENA_BYTES, (name, off, per)
            h = nc.virtual_sbuf_tensor(name, shape, dtype)
            nc._tensor(h.name, h.shape, h.dtype, type="SB")
            mloc = nc.lookup_mloc(h)
            mloc.allocated = True
            mloc.addr = base + off
            return h

        pswp = place("pswp", [128, 128], F32, 0)
        bq = place("bq", [128, NH], F32, 512)
        bo = place("bo", [128, NH], F32, 576)
        bk = place("bk", [128, 1], F32, 640)
        irow = place("irow", [128, 1], F32, 672)
        sgnc = place("sgnc", [128, 1], F32, 704)
        lnb = place("lnb", [128, 1], F32, 736)
        invf = place("invf", [128, 1], F32, 768)
        invsq = place("invsq", [128, 1], F32, 800)
        onescol_bf = place("onescol_bf", [128, 1], BF16, 832)
        onescol_f = place("onescol_f", [128, 1], F32, 864)
        accg = [place(f"accg{i}", [128, 1], F32, 896 + 32 * i) for i in range(3)]
        red128 = place("red128", [128, 1], F32, 1024)
        thp = [place(f"thp{i}", [128, 1], F32, 1056 + 32 * i) for i in range(3)]
        thn = [place(f"thn{i}", [128, 1], F32, 1152 + 32 * i) for i in range(3)]
        bvr = place("bvr", [1, 128], F32, 1248)
        onesrow_bf = place("onesrow_bf", [1, 128], BF16, 1760)
        onesrow_f = place("onesrow_f", [1, 128], F32, 2048)
        grow = place("grow", [1, 64], F32, 2560)
        rn_col = place("rn_col", [128, 1], F32, 2880)
        rinv_col = place("rinv_col", [128, 1], F32, 2912)
        c10k = place("c10k", [1, 1], F32, 2816)
        zrow = place("zrow", [1, 512], F32, 19 * KB)
        sqhd = place("sqhd", [1, 1], F32, 2848)
        srow = place("srowA", [1, 2048], F32, 3 * KB)
        srow2 = place("srowB", [1, 2048], F32, 11 * KB)
        inv_bf = place("inv_bf", [1, Q], BF16, 19 * KB)
        tmpA = place("tmpA", [128, Q], F32, 21 * KB)
        sgtmp = [place(f"sgtmp{i}", [128, 512], BF16, 19 * KB + KB * i) for i in range(2)]
        redcols = place("redcols", [128, 288], F32, 19 * KB)

        A0 = 25 * KB
        B0 = 57 * KB
        C0 = 89 * KB
        D0 = 121 * KB
        E0 = 153 * KB
        F0 = 169 * KB
        G0 = 177 * KB
        TOP = 185 * KB

        t_u = place("t_u", [128, T], F32, A0)
        t_ki = place("t_ki", [128, T], I32, A0 + 8 * KB)
        t_kf = place("t_kf", [128, T], F32, A0 + 16 * KB)
        t_kf2 = place("t_kf2", [128, T], F32, A0 + 24 * KB)
        t_r0 = place("t_r0", [128, T], F32, B0)
        t_work = place("t_work", [128, T], F32, B0 + 8 * KB)
        t_pos = place("t_pos", [128, T], F32, B0 + 16 * KB)
        t_posq = place("t_posq", [128, Q], F32, B0 + 24 * KB)
        t_ang = place("t_ang", [128, T], F32, C0)
        t_sg = place("t_sg", [128, T], F32, C0 + 8 * KB)
        t_sraw = place("t_sraw", [128, T], F32, C0 + 16 * KB)

        xo_bf = [place(f"xo_bf{i}", [128, Q], BF16, A0 + 2 * KB * i) for i in range(DC)]
        q_rot = [place(f"q_rot{i}", [128, Q], BF16, B0 + 2 * KB * i) for i in range(DC)]
        q_sb = [place(f"q_sb{i}", [128, Q], F32, C0 + 4 * KB * i) for i in range(4)]
        xf_st = [place(f"xf_st{i}", [128, T], F32, C0 + 8 * KB * i) for i in range(2)]
        k_sb = place("k_sb", [128, T], F32, C0 + 16 * KB)
        p_sb = [place(f"p_sb{i}", [128, Q], BF16, C0 + 2 * KB * i) for i in range(16)]
        r1b = [place(f"r1b{i}", [128, Q], BF16, C0 + 2 * KB * i) for i in range(DC)]
        ctx_bf = [place(f"ctxb{i}", [128, Q], BF16, D0 + 2 * KB * i) for i in range(DC)]
        r2b = [place(f"r2b{i}", [128, Q], BF16, D0 + 2 * KB * i) for i in range(DC)]
        cosf = place("cosf", [128, T], F32, D0)
        sinf = place("sinf", [128, T], F32, D0 + 8 * KB)
        wblk = [place(f"wblk{i}", [128, 512], BF16, E0 + KB * i) for i in range(DC)]
        sqt = [place(f"sqt{i}", [128, Q], BF16, E0 + 2 * KB * i) for i in range(2)]
        w3q = [place(f"w3q{i}", [128, 512], BF16,
                     (E0 + KB * i) if i < 16 else (C0 + 26 * KB + KB * (i - 16)))
               for i in range(22)]
        x_bf = [place(f"x_bf{i}", [128, T], BF16, F0 + 4 * KB * i) for i in range(2)]
        xo_st = [place(f"xo_st{i}", [128, Q], F32, F0 + 4 * KB * i) for i in range(2)]
        y_sb = [place(f"y_sb{i}", [128, Q], F32, F0 + 4 * KB * i) for i in range(2)]
        sig_bf = [place(f"sig{i}", [128, Q], BF16, F0 + 2 * KB * i) for i in range(4)]
        cosq = place("cosq", [128, Q], F32, G0)
        sinq = place("sinq", [128, Q], F32, G0 + 4 * KB)
        k_rot = place("k_rot", [128, T], BF16, G0)
        v_tm = [place(f"v_tm{i}", [128, HD], BF16, G0 + 4 * KB + 256 * i) for i in range(16)]
        gst = [place(f"gst{i}", [128, Q], F32, C0 + 4 * KB * i) for i in range(2)]
        xst3 = [place(f"xst3{i}", [128, Q], F32, G0 + 4 * KB * i) for i in range(2)]
        wstA = [place(f"wstA{i}", [128, 512], F32, C0 + 16 * KB + 2 * KB * i) for i in range(2)]
        wstB = [place(f"wstB{i}", [128, 512], F32, F0 + 2 * KB * i) for i in range(2)]
        wstC = [place(f"wstC{i}", [128, 512], F32, G0 + 2 * KB * i) for i in range(2)]
        w3st = [place(f"w3st{i}", [128, 512], F32, G0 + 2 * KB * i) for i in range(2)]
        t_r1 = tmpA
        hidden = []
        for i in range(HC):
            if i < 16:
                off = A0 + 2 * KB * i
            elif i < 32:
                off = B0 + 2 * KB * (i - 16)
            else:
                off = C0 + 2 * KB * (i - 32)
            hidden.append(place(f"hid{i}", [128, Q], BF16, off))
        assert TOP <= ARENA_BYTES

        PA = ctx.enter_context(nc.psum_tensor("PA", [128, 1024], F32))
        PB = ctx.enter_context(nc.psum_tensor("PB", [128, 1024], F32))
        PC = ctx.enter_context(nc.psum_tensor("PC", [128, 1024], F32))
        PD = ctx.enter_context(nc.psum_tensor("PD", [128, 1024], F32))
        PS4 = [PA, PB, PC, PD]

        sems = {}
        for s in ("bar", "ds", "pe", "act", "dve", "gp", "out"):
            sems[s] = ctx.enter_context(nc.semaphore(s))

        p = Plan()

        def dma(dst, src, waits=()):
            p.count["dsn"] = p.count.get("dsn", 0) + 1
            p.emit("sync", lambda eng, d=dst, s=src: eng.dma_start(d, s),
                   waits=waits, inc=("ds", 16))
            return 16 * (p.n("dsn") + 1)

        def _dummy():
            p.count["dsn"] = p.count.get("dsn", 0) + 1
            p.emit("sync", lambda eng: eng.dma_start(pswp[0:1, 0:16], PSWP[0:1, 0:16]),
                   inc=("ds", 16))
        p.dummy_dma = _dummy

        def pe(fn, waits=(), inc=True):
            p.emit("tensor", fn, waits=waits, inc=("pe", 1) if inc else None)
            return p.n("pe")

        def sc(fn, waits=(), inc=True):
            p.emit("scalar", fn, waits=waits, inc=("act", 1) if inc else None)
            return p.n("act")

        def ve(fn, waits=(), inc=True):
            p.emit("vector", fn, waits=waits, inc=("dve", 1) if inc else None)
            return p.n("dve")

        def gp(fn, waits=(), inc=True):
            p.emit("gpsimd", fn, waits=waits, inc=("gp", 1) if inc else None)
            return p.n("gp")

        def finish_debug(tiles):
            import os as _os
            for e_ in ("gpsimd", "sync"):
                p.wait(e_, "dve", p.n("dve"))
                p.wait(e_, "act", p.n("act"))
                p.wait(e_, "pe", p.n("pe"))
            for i, tile in enumerate(tiles[:16]):
                ncols = tile.shape[-1]
                p.count["outn"] = p.count.get("outn", 0) + 1
                eng_n = "sync" if str(tile.tensor.dtype) == "dt.float32" else "gpsimd"
                p.emit(eng_n, lambda eng, i=i, t=tile, nco=ncols: eng.dma_start(
                    Y[i * 128:(i + 1) * 128, 0:nco], t), inc=("out", 16))
            p.wait("sync", "out", 16 * p.n("outn"))
            p.wait("gpsimd", "out", 16 * p.n("outn"))
            if _os.environ.get("ATH_DUMP_PLAN"):
                for e in ENGINES:
                    print(f"== {e} ({len(p.stream[e])}) ==")
                    for op in p.stream[e]:
                        if op[0] == "w":
                            print(f"  wait {op[1]} >= {op[2]}")
                        else:
                            import inspect as _in
                            try:
                                ln = op[1].__code__.co_firstlineno
                            except Exception:
                                ln = "?"
                            print(f"  x L{ln} inc={op[2]}")
            with nc.Block() as block:
                def runner(name):
                    def body(eng):
                        for op in p.stream[name]:
                            if op[0] == "w":
                                eng.wait_ge(sems[op[1]], op[2])
                            else:
                                inst = op[1](eng)
                                if op[2] is not None:
                                    inst.then_inc(sems[op[2][0]], op[2][1])
                    return body
                block.sync(runner("sync"))
                block.gpsimd(runner("gpsimd"))
                block.tensor(runner("tensor"))
                block.scalar(runner("scalar"))
                block.vector(runner("vector"))
            return nc

        for dst, src in ((pswp[:], PSWP), (bq[:], BQ), (bo[:], BO), (bk[:], BK),
                         (irow[:], IROW), (sgnc[:], SGNC), (bvr[:], BVR),
                         (t_pos[:], POS), (t_posq[:], POSQ)):
            dma(dst, src)
        dbg8 = None
        if trunc == 8:
            dbg8 = ctx.enter_context(nc.sbuf_tensor("dbg8", [128, 1024], F32))
            dma(dbg8[:], W1T[0:128, 0:1024])
        w_const = 16 * p.n("dsn")
        for tile, val in ((onescol_bf, 1.0), (onescol_f, 1.0), (onesrow_bf, 1.0),
                          (onesrow_f, 1.0), (c10k, 10000.0), (sqhd, SQRT_HD),
                          (accg[0], 0.0), (accg[1], 0.0), (accg[2], 0.0),
                          (zrow, 0.0)):
            w_ms = ve(lambda eng, t=tile, v=val: eng.memset(t[:], v))

        def bcast_col(dst_col, src_11, waits=()):
            wbm = pe(lambda eng, d=dst_col, sr=src_11: eng.matmul(
                PB[:, 0:1], onesrow_f[:], sr, start=True, stop=True),
                waits=tuple(waits) + (("dve", p.n("dve")),))
            return ve(lambda eng, d=dst_col: eng.tensor_copy(d, PB[:, 0:1]),
                      waits=(("pe", wbm),))

        w_ln = sc(lambda eng: eng.activation(grow[0:1, 0:1], c10k[0:1, 0:1], ACT.Ln),
                  waits=(("dve", w_ms),))
        w_b1 = bcast_col(lnb[:, 0:1], grow[0:1, 0:1], waits=(("act", w_ln),))
        w_e = ve(lambda eng: eng.tensor_tensor(lnb[:], irow[:], lnb[:], op=ALU.mult),
                 waits=(("dve", w_b1), ("ds", w_const)))
        w_invf = sc(lambda eng: eng.activation(invf[:], lnb[:], ACT.Exp),
                    waits=(("dve", w_e),))
        w_rs = ve(lambda eng: eng.reciprocal(grow[0:1, 1:2], sqhd[0:1, 0:1]))
        w_b2 = bcast_col(invsq[:, 0:1], grow[0:1, 1:2], waits=(("dve", w_rs),))

        def gen_tables(pos_t, n, cos_out, sin_out, scaled):
            ve(lambda eng, po=pos_t, n=n: eng.tensor_scalar(
                t_ang[:, 0:n], po[:, 0:n], invf[:, 0:1], None, op0=ALU.mult),
               waits=(("act", w_invf), ("dve", w_b2)))

            def cascade(out_ap, k_ap, in_ap):
                ve(lambda eng, o=out_ap, k=k_ap, a=in_ap: eng.scalar_tensor_tensor(
                    o, k, -C1, a, op0=ALU.mult, op1=ALU.add))
                ve(lambda eng, o=out_ap, k=k_ap: eng.scalar_tensor_tensor(
                    o, k, -C2, o, op0=ALU.mult, op1=ALU.add))
                ve(lambda eng, o=out_ap, k=k_ap: eng.scalar_tensor_tensor(
                    o, k, -C3, o, op0=ALU.mult, op1=ALU.add))

            def reduce_pi(out_t, in_t, bias0, addp):
                ve(lambda eng, a=in_t, b=bias0, n=n: eng.tensor_scalar(
                    t_u[:, 0:n], a[:, 0:n], INVPI, float(b), op0=ALU.mult, op1=ALU.add))
                ve(lambda eng, n=n: eng.tensor_copy(t_ki[:, 0:n], t_u[:, 0:n]))
                ve(lambda eng, n=n: eng.tensor_copy(t_kf[:, 0:n], t_ki[:, 0:n]))
                cascade(t_r0[:, 0:n], t_kf[:, 0:n], in_t[:, 0:n])
                if addp:
                    ve(lambda eng, n=n: eng.tensor_scalar(
                        t_r0[:, 0:n], t_r0[:, 0:n], P1, P2, op0=ALU.add, op1=ALU.add))
                ve(lambda eng, n=n: eng.tensor_scalar(
                    t_u[:, 0:n], t_r0[:, 0:n], INVPI, None, op0=ALU.mult))
                ve(lambda eng, n=n: eng.tensor_copy(t_ki[:, 0:n], t_u[:, 0:n]))
                ve(lambda eng, n=n: eng.tensor_copy(t_kf2[:, 0:n], t_ki[:, 0:n]))
                ve(lambda eng, n=n: eng.scalar_tensor_tensor(
                    t_u[:, 0:n], t_kf2[:, 0:n], -1.0, t_u[:, 0:n],
                    op0=ALU.mult, op1=ALU.add))
                ve(lambda eng, n=n: eng.tensor_scalar(
                    t_work[:, 0:n], t_u[:, 0:n], 0.5, None, op0=ALU.is_gt))
                ve(lambda eng, n=n: eng.tensor_scalar(
                    t_u[:, 0:n], t_u[:, 0:n], -0.5, None, op0=ALU.is_lt))
                ve(lambda eng, n=n: eng.tensor_tensor(
                    t_work[:, 0:n], t_work[:, 0:n], t_u[:, 0:n], op=ALU.subtract))
                ve(lambda eng, n=n: eng.tensor_tensor(
                    t_kf2[:, 0:n], t_kf2[:, 0:n], t_work[:, 0:n], op=ALU.add))
                ve(lambda eng, n=n: eng.tensor_copy(t_ki[:, 0:n], t_kf2[:, 0:n]))
                ve(lambda eng, n=n: eng.tensor_copy(t_kf2[:, 0:n], t_ki[:, 0:n]))
                cascade(out_t[:, 0:n], t_kf2[:, 0:n], t_r0[:, 0:n])
                ve(lambda eng, n=n: eng.tensor_tensor(
                    t_sg[:, 0:n], t_kf[:, 0:n], t_kf2[:, 0:n], op=ALU.add))
                ve(lambda eng, n=n: eng.tensor_copy(t_ki[:, 0:n], t_sg[:, 0:n]))
                ve(lambda eng, n=n: eng.tensor_scalar(
                    t_ki[:, 0:n], t_ki[:, 0:n], 1, None, op0=ALU.bitwise_and))
                ve(lambda eng, n=n: eng.tensor_copy(t_sg[:, 0:n], t_ki[:, 0:n]))
                return ve(lambda eng, n=n: eng.tensor_scalar(
                    t_sg[:, 0:n], t_sg[:, 0:n], -2.0, 1.0, op0=ALU.mult, op1=ALU.add))

            for is_cos in (False, True):
                wv = reduce_pi(t_work, t_ang, 1.0 if is_cos else 0.5, is_cos)
                wsin = sc(lambda eng, n=n: eng.activation(
                    t_sraw[:, 0:n], t_work[:, 0:n], ACT.Sin), waits=(("dve", wv),))
                if not is_cos:
                    ve(lambda eng, n=n: eng.tensor_scalar(
                        t_sg[:, 0:n], t_sg[:, 0:n], sgnc[:, 0:1], None, op0=ALU.mult))
                if scaled:
                    ve(lambda eng, n=n: eng.tensor_scalar(
                        t_sg[:, 0:n], t_sg[:, 0:n], invsq[:, 0:1], None, op0=ALU.mult))
                out_t = cos_out if is_cos else sin_out
                ve(lambda eng, o=out_t, n=n: eng.tensor_tensor(
                    o[:, 0:n], t_sraw[:, 0:n], t_sg[:, 0:n], op=ALU.mult),
                   waits=(("act", wsin),))

        if ph_on('t'):
            gen_tables(t_pos, T, cosf, sinf, False)
            gen_tables(t_posq, Q, cosq, sinq, True)
        p.barrier()
        if trunc == 5:
            return finish_debug([cosf[:, 0:1024], cosf[:, 1024:2048],
                                 sinf[:, 0:1024], sinf[:, 1024:2048],
                                 cosq[:], sinq[:], t_ang[:, 0:1024],
                                 t_work[:, 0:1024], t_sg[:, 0:1024],
                                 t_kf[:, 0:1024], t_kf2[:, 0:1024],
                                 t_u[:, 0:1024]])

        xo_ready = [0] * DC
        for i in range(DC) if ph_on('qb') else []:
            wd = dma(xo_st[i % 2][:], XO[i * 128:(i + 1) * 128, :],
                     waits=((("act", xo_ready[i - 2]),) if i >= 2 else ()))
            xo_ready[i] = sc(lambda eng, i=i: eng.copy(xo_bf[i][:], xo_st[i % 2][:]),
                             waits=(("ds", wd),))

        qswap_done = [0, 0, 0, 0]
        wmm_prev = 0
        stA = [("act", 0), ("act", 0)]
        for g in range(4) if ph_on('qb') else []:
            wq_ready = [0] * DC
            for i in range(DC):
                sl, wv = stA[i % 2]
                wd = dma(wstA[i % 2][:], WQ[i * 128:(i + 1) * 128, g * 512:(g + 1) * 512],
                         waits=((sl, wv),))
                wq_ready[i] = sc(lambda eng, i=i: eng.copy(wblk[i][:], wstA[i % 2][:]),
                                 waits=(("ds", wd), ("pe", wmm_prev)))
                stA[i % 2] = ("act", wq_ready[i])
            for i in range(DC):
                for j in range(4):
                    for hf in range(2):
                        pe(lambda eng, i=i, j=j, hf=hf: eng.matmul(
                            PS4[j][:, hf * 512:(hf + 1) * 512],
                            wblk[i][:, j * 128:(j + 1) * 128],
                            xo_bf[i][:, hf * 512:(hf + 1) * 512],
                            start=(i == 0), stop=(i == DC - 1)),
                           waits=(("act", max(wq_ready[i], xo_ready[i])),
                                  ("dve", qswap_done[j])),
                           inc=(i == DC - 1 and j == 3 and hf == 1))
            wmm = p.n("pe")
            wmm_prev = wmm
            ev = []
            for j in range(4):
                hh = g * 4 + j
                ev.append(sc(lambda eng, j=j, hh=hh: eng.activation(
                    q_sb[j][:], PS4[j][:], ACT.Identity, bias=bq[:, hh:hh + 1]),
                    waits=(("pe", wmm),)))
            for j in range(4):
                hh = g * 4 + j
                for hf in range(2):
                    wsw = pe(lambda eng, j=j, hf=hf: eng.matmul(
                        PS4[j][:, hf * 512:(hf + 1) * 512], pswp[:],
                        q_sb[j][:, hf * 512:(hf + 1) * 512], start=True, stop=True),
                        waits=(("act", ev[j]),), inc=(hf == 1))
                ve(lambda eng, j=j: eng.tensor_tensor(
                    tmpA[:], PS4[j][:], sinq[:], op=ALU.mult),
                   waits=(("pe", wsw),))
                ve(lambda eng, j=j: eng.tensor_tensor(
                    q_sb[j][:], q_sb[j][:], cosq[:], op=ALU.mult))
                qswap_done[j] = ve(lambda eng, j=j, hh=hh: eng.tensor_tensor(
                    q_rot[hh][:], q_sb[j][:], tmpA[:], op=ALU.add))
        p.barrier()

        if trunc == 3:
            return finish_debug([q_rot[i][:] for i in range(16)])
        kv_ready = [0] * DC
        kvt = [max(stA[0][1], stA[1][1])] * 2
        for i in range(DC) if ph_on('kv') else []:
            wd1 = dma(wstA[i % 2][:, 0:128], WK[i * 128:(i + 1) * 128, :],
                      waits=(("act", kvt[i % 2]),))
            wd2 = dma(wstA[i % 2][:, 128:256], WV[i * 128:(i + 1) * 128, :])
            sc(lambda eng, i=i: eng.copy(wblk[i][:, 0:128], wstA[i % 2][:, 0:128]),
               waits=(("ds", wd2),))
            kvt[i % 2] = sc(lambda eng, i=i: eng.copy(
                wblk[i][:, 128:256], wstA[i % 2][:, 128:256]))
            kv_ready[i] = kvt[i % 2]

        for bi in range(4) if ph_on('kv') else []:
            dst = PC if bi < 2 else PD
            pe(lambda eng, dst=dst, bi=bi: eng.matmul(
                dst[:, (bi % 2) * 512:(bi % 2 + 1) * 512], onesrow_f[:], zrow[:],
                start=True, stop=False), inc=False)
        x_ready = [0] * DC
        x_used = [0] * DC
        for i in range(DC) if ph_on('kv') else []:
            wd = dma(xf_st[i % 2][:], XF[i * 128:(i + 1) * 128, :],
                     waits=((("act", x_ready[i - 2]),) if i >= 2 else ()))
            x_ready[i] = sc(lambda eng, i=i: eng.copy(x_bf[i % 2][:], xf_st[i % 2][:]),
                            waits=(("ds", wd),
                                   (("pe", x_used[i - 2]) if i >= 2 else ("pe", 0))))
            for piece in range(4):
                dst = (PA if piece < 2 else PB)
                pe(lambda eng, i=i, piece=piece, dst=dst: eng.matmul(
                    dst[:, (piece % 2) * 512:(piece % 2 + 1) * 512],
                    wblk[i][:, 0:128],
                    x_bf[i % 2][:, piece * 512:(piece + 1) * 512],
                    start=(i == 0), stop=(i == DC - 1)),
                   waits=(("act", max(kv_ready[i], x_ready[i])),), inc=False)
            for tc in range(16):
                dst = PC if tc < 8 else PD
                co = (tc % 8) * 128
                pe(lambda eng, i=i, tc=tc, dst=dst, co=co: eng.matmul(
                    dst[:, co:co + 128],
                    x_bf[i % 2][:, tc * 128:(tc + 1) * 128],
                    wblk[i][:, 128:256],
                    start=False, stop=False),
                   inc=False)
            x_used[i] = pe(lambda eng: eng.drain(), inc=True)
        for tc in (range(16) if ph_on('kv') else []):
            dst = PC if tc < 8 else PD
            co = (tc % 8) * 128
            pe(lambda eng, dst=dst, co=co, tc=tc: eng.matmul(
                dst[:, co:co + 128], onesrow_f[:], bvr[:],
                start=False, stop=(tc % 4 == 3)), inc=(tc == 15))
        w_kv = p.n("pe")

        if not ph_on('kv'):
            p.barrier()
        w_kev1 = 0 if not ph_on('kv') else sc(lambda eng: eng.activation(
            k_sb[:, 0:1024], PA[:], ACT.Identity, bias=bk[:, 0:1]),
            waits=(("pe", w_kv),))
        w_kev2 = 0 if not ph_on('kv') else sc(lambda eng: eng.activation(
            k_sb[:, 1024:2048], PB[:], ACT.Identity, bias=bk[:, 0:1]))
        vev = 0
        for tc in (range(16) if ph_on('kv') else []):
            src = PC if tc < 8 else PD
            co = (tc % 8) * 128
            vev = ve(lambda eng, tc=tc, src=src, co=co: eng.tensor_copy(
                v_tm[tc][:], src[:, co:co + 128]),
                waits=(("pe", w_kv),))
        for hf in (range(2) if ph_on('kv') else []):
            w_ksw = pe(lambda eng, hf=hf, dst=PC: eng.matmul(
                PC[:, hf * 512:(hf + 1) * 512], pswp[:],
                k_sb[:, hf * 512:(hf + 1) * 512], start=True, stop=True),
                waits=(("dve", vev), ("act", w_kev2)), inc=(hf == 1))
        for hf in (range(2) if ph_on('kv') else []):
            w_ksw2 = pe(lambda eng, hf=hf: eng.matmul(
                PD[:, hf * 512:(hf + 1) * 512], pswp[:],
                k_sb[:, 1024 + hf * 512:1024 + (hf + 1) * 512],
                start=True, stop=True), inc=(hf == 1))
        for half, src, wp in (((0, PC, w_ksw), (1, PD, w_ksw2)) if ph_on('kv') else ()):
            o = half * 1024
            ve(lambda eng, src=src, o=o: eng.tensor_tensor(
                tmpA[:], src[:], sinf[:, o:o + 1024], op=ALU.mult),
               waits=(("pe", wp),))
            ve(lambda eng, o=o: eng.tensor_tensor(
                k_sb[:, o:o + 1024], k_sb[:, o:o + 1024], cosf[:, o:o + 1024],
                op=ALU.mult))
            ve(lambda eng, o=o: eng.tensor_tensor(
                k_rot[:, o:o + 1024], k_sb[:, o:o + 1024], tmpA[:], op=ALU.add))
        p.barrier()

        if trunc == 4:
            tiles = [k_rot[:, 0:1024], k_rot[:, 1024:2048]]
            vrow = []
            for tc in range(8):
                vrow.append(v_tm[tc][:])
            tiles += [v_tm[tc][:] for tc in range(8)]
            return finish_debug(tiles)
        exp_w = [0] * (NH * 16)
        ctx_evac_w = 0
        invb_copy_w = 0
        for h in (range(NH) if ph_on('att') else []):
            for kc in range(16):
                buf = PA if kc % 2 == 0 else PB
                waits = []
                gi = h * 16 + kc - 2
                if gi >= 0:
                    waits.append(("act", exp_w[gi]))
                for hf in range(2):
                    wsc = pe(lambda eng, kc=kc, h=h, hf=hf, buf=buf: eng.matmul(
                        buf[:, hf * 512:(hf + 1) * 512],
                        k_rot[:, kc * 128:(kc + 1) * 128],
                        q_rot[h][:, hf * 512:(hf + 1) * 512],
                        start=True, stop=True),
                        waits=waits, inc=(hf == 1))
                    waits = []
                exp_w[h * 16 + kc] = sc(lambda eng, kc=kc, buf=buf: eng.activation(
                    p_sb[kc][:], buf[:], ACT.Exp), waits=(("pe", wsc),))
            for kc in range(16):
                for hf in range(2):
                    wsum = pe(lambda eng, kc=kc, hf=hf: eng.matmul(
                        PD[0:1, hf * 512:(hf + 1) * 512], onescol_bf[:],
                        p_sb[kc][:, hf * 512:(hf + 1) * 512],
                        start=(kc == 0), stop=(kc == 15)),
                        waits=(("act", exp_w[h * 16 + kc]),
                               ("act", invb_copy_w)),
                        inc=(kc == 15 and hf == 1))
            w_rec = ve(lambda eng: eng.reciprocal(srow[0:1, 0:1024], PD[0:1, :]),
                       waits=(("pe", wsum),))
            for hf in range(2):
                w_invb = pe(lambda eng, hf=hf: eng.matmul(
                    PD[:, hf * 512:(hf + 1) * 512], onesrow_f[:],
                    srow[0:1, hf * 512:(hf + 1) * 512], start=True, stop=True),
                    waits=(("dve", w_rec),), inc=(hf == 1))
            invb_copy_w = sc(lambda eng: eng.copy(tmpA[:], PD[:]),
                             waits=(("pe", w_invb),))
            for kc in range(16):
                for hf in range(2):
                    wctx = pe(lambda eng, kc=kc, hf=hf: eng.matmul(
                        PC[:, hf * 512:(hf + 1) * 512], v_tm[kc][:],
                        p_sb[kc][:, hf * 512:(hf + 1) * 512],
                        start=(kc == 0), stop=(kc == 15)),
                        waits=(("dve", ctx_evac_w),), inc=(kc == 15 and hf == 1))
            ctx_evac_w = ve(lambda eng, h=h: eng.tensor_tensor(
                ctx_bf[h][:], PC[:], tmpA[:], op=ALU.mult),
                waits=(("pe", wctx), ("act", invb_copy_w)))
        p.barrier()

        if trunc == 2:
            return finish_debug([ctx_bf[i][:] for i in range(16)])
        r1_w = 0
        xst3_tail = [0, 0]
        wo_mm_prev = 0
        stB = [0, 0]
        for g in (range(4) if ph_on('wo') else []):
            wo_ready = [0] * DC
            for i in range(DC):
                wd = dma(wstB[i % 2][:], WO[i * 128:(i + 1) * 128, g * 512:(g + 1) * 512],
                         waits=(("act", stB[i % 2]),))
                wo_ready[i] = sc(lambda eng, i=i: eng.copy(wblk[i][:], wstB[i % 2][:]),
                                 waits=(("ds", wd), ("pe", wo_mm_prev)))
                stB[i % 2] = wo_ready[i]
            for j in range(4):
                do = g * 4 + j
                buf = PS4[j]
                for i in range(DC):
                    for hf in range(2):
                        wmm = pe(lambda eng, i=i, j=j, hf=hf, buf=buf: eng.matmul(
                            buf[:, hf * 512:(hf + 1) * 512],
                            wblk[i][:, j * 128:(j + 1) * 128],
                            ctx_bf[i][:, hf * 512:(hf + 1) * 512],
                            start=(i == 0), stop=(i == DC - 1)),
                            waits=(("act", wo_ready[i]), ("dve", r1_w)),
                            inc=(i == DC - 1 and hf == 1))
                wx = dma(xst3[do % 2][:], XO[do * 128:(do + 1) * 128, :],
                         waits=(("dve", xst3_tail[do % 2]),))
                wev = sc(lambda eng, do=do, buf=buf: eng.activation(
                    t_r1[:], buf[:], ACT.Identity, bias=bo[:, do:do + 1]),
                    waits=(("pe", wmm), ("dve", r1_w)))
                r1_w = ve(lambda eng, do=do: eng.tensor_tensor(
                    r1b[do][:], t_r1[:], xst3[do % 2][:], op=ALU.add),
                    waits=(("act", wev), ("ds", wx)))
                xst3_tail[do % 2] = r1_w
            wo_mm_prev = p.n("pe")
        p.barrier()

        def layer_norm(src, halve, out_writer):
            wsq = 0
            for i in range(DC):
                wsq_i = ve(lambda eng, i=i: eng.tensor_tensor(
                    sqt[i % 2][:], src[i][:], src[i][:], op=ALU.mult),
                    waits=(("pe", wsq),))
                for hf in range(2):
                    pe(lambda eng, i=i, hf=hf: eng.matmul(
                        PC[0:1, hf * 512:(hf + 1) * 512], onescol_bf[:],
                        src[i][:, hf * 512:(hf + 1) * 512],
                        start=(i == 0), stop=(i == DC - 1)), inc=False)
                for hf in range(2):
                    wsq = pe(lambda eng, i=i, hf=hf: eng.matmul(
                        PD[0:1, hf * 512:(hf + 1) * 512], onescol_bf[:],
                        sqt[i % 2][:, hf * 512:(hf + 1) * 512],
                        start=(i == 0), stop=(i == DC - 1)),
                        waits=(("dve", wsq_i),),
                        inc=(hf == 1))
            s_m = srow[0:1, 0:1024]
            s_q = srow[0:1, 1024:2048]
            s_a = srow2[0:1, 0:1024]
            s_b = srow2[0:1, 1024:2048]
            ve(lambda eng: eng.tensor_scalar(s_m, PC[0:1, :], 1.0 / 2048.0, None,
                                             op0=ALU.mult), waits=(("pe", wsq),))
            ve(lambda eng: eng.tensor_scalar(s_q, PD[0:1, :], 1.0 / 2048.0, None,
                                             op0=ALU.mult))
            ve(lambda eng: eng.tensor_tensor(s_b, s_m, s_m, op=ALU.mult))
            ve(lambda eng: eng.tensor_tensor(s_q, s_q, s_b, op=ALU.subtract))
            wv = ve(lambda eng: eng.tensor_scalar(s_q, s_q, EPS_LN, None, op0=ALU.add))
            wsqrt = sc(lambda eng: eng.activation(s_b, s_q, ACT.Sqrt),
                       waits=(("dve", wv),))
            ve(lambda eng: eng.reciprocal(s_a, s_b), waits=(("act", wsqrt),))
            ve(lambda eng: eng.tensor_tensor(s_b, s_a, s_a, op=ALU.mult))
            ve(lambda eng: eng.tensor_tensor(s_b, s_b, s_q, op=ALU.mult))
            ve(lambda eng: eng.tensor_scalar(s_b, s_b, -0.5, 1.5, op0=ALU.mult,
                                             op1=ALU.add))
            ve(lambda eng: eng.tensor_tensor(s_a, s_a, s_b, op=ALU.mult))
            if halve:
                ve(lambda eng: eng.tensor_scalar(s_a, s_a, 0.5, None, op0=ALU.mult))
            wab = ve(lambda eng: eng.scalar_tensor_tensor(
                s_b, s_m, -1.0, s_a, op0=ALU.mult, op1=ALU.mult))
            for hf in range(2):
                pe(lambda eng, hf=hf: eng.matmul(
                    PA[:, hf * 512:(hf + 1) * 512], onesrow_f[:],
                    s_a[0:1, hf * 512:(hf + 1) * 512], start=True, stop=True),
                   waits=(("dve", wab),), inc=False)
            wbc = 0
            for hf in range(2):
                wbc = pe(lambda eng, hf=hf: eng.matmul(
                    PB[:, hf * 512:(hf + 1) * 512], onesrow_f[:],
                    s_b[0:1, hf * 512:(hf + 1) * 512], start=True, stop=True),
                    inc=(hf == 1))
            wlast = 0
            for i in range(DC):
                ve(lambda eng, i=i: eng.tensor_tensor(
                    t_r1[:], src[i][:], PA[:], op=ALU.mult),
                   waits=(("pe", wbc),))
                wlast = out_writer(i)
            return wlast

        def ln1_writer(i):
            return ve(lambda eng, i=i: eng.tensor_tensor(
                r2b[i][:], t_r1[:], PB[:], op=ALU.add))
        if ph_on('ln1'):
            layer_norm(r1b, True, ln1_writer)
        p.barrier()

        if trunc:
            p.wait("sync", "out", 0)
        gst_tail = [0, 0]
        if trunc == 10:
            ve(lambda eng: eng.memset(tmpA[:], 0.0))

        def gamma_pass(W, nrows, acc, mq=0):
            if trunc == 1:
                return 0
            slices = []
            for r0 in range(0, nrows, 128):
                ncols = HID if W is not W3T else D
                for c0 in range(0, ncols, 1024):
                    cw = min(1024, ncols - c0)
                    slices.append((r0, c0, cw))
            nsl = len(slices)
            base = mq * 96
            for si, (r0, c0, cw) in enumerate(slices):
                wd = dma(gst[si % 2][:, 0:cw], W[r0:r0 + 128, c0:c0 + cw],
                         waits=(("dve", gst_tail[si % 2]),))
                ve(lambda eng, si=si, cw=cw, base=base: eng.tensor_reduce(
                    redcols[:, base + si:base + si + 1], gst[si % 2][:, 0:cw],
                    axis=mybir.AxisListType.X,
                    op=ALU.add, apply_absolute_value=True),
                   waits=(("ds", wd),))
                wsac = ve(lambda eng, si=si, base=base: eng.tensor_copy(
                    red128[:], redcols[:, base + si:base + si + 1]))
                gst_tail[si % 2] = wsac
                if trunc == 10 and si >= nsl - 8:
                    cix = mq * 10 + (si - (nsl - 8))
                    ve(lambda eng, cix=cix, si=si, base=base: eng.tensor_copy(
                        tmpA[:, cix:cix + 1], redcols[:, base + si:base + si + 1]))
            return nsl

        nsls = []
        for m, (W, nrows, acc) in enumerate(((W1T, D, accg[0]), (W2T, D, accg[1]),
                                             (W3T, HID, accg[2]))):
            nsls.append(gamma_pass(W, nrows, acc, mq=m))
        if trunc != 1:
            p.barrier()
        wg = []
        for m, acc in enumerate(accg):
            if trunc == 1:
                wg.append(0)
                continue
            wg.append(ve(lambda eng, acc=acc, m=m: eng.tensor_reduce(
                acc[:], redcols[:, m * 96:m * 96 + nsls[m]],
                axis=mybir.AxisListType.X, op=ALU.add)))
            if trunc == 10:
                ve(lambda eng, acc=acc, m=m: eng.tensor_copy(
                    tmpA[:, m * 10 + 8:m * 10 + 9], acc[:]))
        if trunc == 10:
            p.barrier()
            for m, acc in enumerate(accg):
                ve(lambda eng, acc=acc, m=m: eng.tensor_copy(
                    tmpA[:, 30 + m:31 + m], acc[:]))
            return finish_debug([tmpA[:], redcols[:]])
        if trunc == 1:
            for i in range(DC):
                p.count["outn"] = p.count.get("outn", 0) + 1
                p.emit("gpsimd", lambda eng, i=i: eng.dma_start(
                    Y[i * 128:(i + 1) * 128, :], r2b[i][:]), inc=("out", 16))
            p.wait("gpsimd", "out", 16 * DC)
            with nc.Block() as block:
                def runner(name):
                    def body(eng):
                        for op in p.stream[name]:
                            if op[0] == "w":
                                eng.wait_ge(sems[op[1]], op[2])
                            else:
                                inst = op[1](eng)
                                if op[2] is not None:
                                    inst.then_inc(sems[op[2][0]], op[2][1])
                    return body
                block.sync(runner("sync"))
                block.gpsimd(runner("gpsimd"))
                block.tensor(runner("tensor"))
                block.scalar(runner("scalar"))
                block.vector(runner("vector"))
            return nc
        wth = []
        g512f = place("g512f", [128, 512], F32, E0)
        g512b = place("g512b", [128, 512], F32, E0 + 2 * KB)
        g512c = place("g512c", [128, 512], F32, E0 + 4 * KB)
        ones128f = place("ones128f", [128, 128], F32, E0 + 6 * KB)
        ve(lambda eng: eng.memset(g512f[:], 0.0))
        ve(lambda eng: eng.memset(ones128f[:], 1.0))
        wcp = 0
        for m in range(3):
            wcp = ve(lambda eng, m=m: eng.tensor_copy(
                g512f[:, m:m + 1], accg[m][:]), waits=(("dve", max(wg)),))
        wt = pe(lambda eng: eng.matmul(PA[:, 0:512], ones128f[:], g512f[:],
                                       start=True, stop=True),
                waits=(("dve", wcp),))
        wcpy = sc(lambda eng: eng.copy(g512f[:, 0:512], PA[:, 0:512]),
                  waits=(("pe", wt),))
        C_INV = float(np.float32(1.0 / N_FFN_W))
        ve(lambda eng: eng.tensor_scalar(
            g512b[:], g512f[:], C_INV, None, op0=ALU.mult),
           waits=(("act", wcpy),))
        ve(lambda eng: eng.scalar_tensor_tensor(
            g512c[:], g512b[:], -N_FFN_W, g512f[:], op0=ALU.mult, op1=ALU.add))
        ve(lambda eng: eng.scalar_tensor_tensor(
            g512b[:], g512c[:], C_INV, g512b[:], op0=ALU.mult, op1=ALU.add))
        ve(lambda eng: eng.tensor_scalar(
            g512b[:], g512b[:], EPS_Q, 0.5, op0=ALU.add, op1=ALU.mult))
        wneg = ve(lambda eng: eng.tensor_scalar(
            g512c[:], g512b[:], -1.0, None, op0=ALU.mult))
        for m in range(3):
            ve(lambda eng, m=m: eng.tensor_copy(thp[m][:, 0:1], g512b[:, m:m + 1]),
               waits=(("dve", wneg),))
            wth.append(ve(lambda eng, m=m: eng.tensor_copy(
                thn[m][:, 0:1], g512c[:, m:m + 1])))
        p.barrier()

        if trunc == 6:
            ve(lambda eng: eng.memset(tmpA[:], 0.0))
            for j, col in enumerate([thp[0], thn[0], thp[1], thn[1], thp[2],
                                     thn[2], accg[0], accg[1], accg[2]]):
                ve(lambda eng, j=j, c=col: eng.tensor_copy(tmpA[:, j:j + 1], c[:]))
            for j_ in range(3):
                ve(lambda eng, j_=j_: eng.tensor_copy(
                    tmpA[:, 10 + j_:11 + j_], g512f[:, j_:j_ + 1]))
            ve(lambda eng: eng.tensor_copy(tmpA[:, 13:14], rinv_col[:]))
            ve(lambda eng: eng.tensor_copy(tmpA[:, 14:15], ones128f[:, 5:6]))
            return finish_debug([tmpA[:]])
        NHB = 11
        _r9 = [None]
        sig_used = [0, 0, 0, 0]
        x2_used = [0, 0, 0, 0]
        wq_done = 0
        wblk_rd = [0] * DC
        stC = [("act", 0), ("act", 0)]
        for hb in range(NHB):
            c0 = hb * 512
            prev_sg = 0
            for i in range(DC):
                wd = dma(wstC[i % 2][:], W1T[i * 128:(i + 1) * 128, c0:c0 + 512],
                         waits=(stC[i % 2],))
                sc(lambda eng, i=i: eng.sign(sgtmp[0][:], wstC[i % 2][:],
                                             bias=thn[0][:, 0:1]),
                   waits=(("ds", wd), ("dve", wq_done)))
                prev_sg = sc(lambda eng, i=i: eng.sign(sgtmp[1][:], wstC[i % 2][:],
                                                       bias=thp[0][:, 0:1]))
                stC[i % 2] = ("act", prev_sg)
                wq_done = ve(lambda eng, i=i: eng.tensor_tensor(
                    wblk[i][:], sgtmp[0][:], sgtmp[1][:], op=ALU.add),
                    waits=(("act", prev_sg), ("pe", wblk_rd[i])))
                for j in range(4):
                    hc = hb * 4 + j
                    for hf in range(2):
                        wx1 = pe(lambda eng, i=i, j=j, hf=hf: eng.matmul(
                            PS4[j][:, hf * 512:(hf + 1) * 512],
                            wblk[i][:, j * 128:(j + 1) * 128],
                            r2b[i][:, hf * 512:(hf + 1) * 512],
                            start=(i == 0), stop=(i == DC - 1)),
                            waits=(("dve", max(wq_done, x2_used[j])),),
                            inc=(i == DC - 1 and j == 3 and hf == 1))
                wblk_rd[i] = p.n("pe")
            wsg = []
            for j in range(4):
                wsg.append(sc(lambda eng, j=j: eng.activation(
                    sig_bf[j][:], PS4[j][:], ACT.Sigmoid),
                    waits=(("pe", wx1), ("dve", sig_used[j]))))
            if trunc == 9 and hb == 0:
                tiles = [wblk[i_][:] for i_ in range(16)]
                ve(lambda eng: eng.memset(tmpA[:], 0.0))
                for j_, col in enumerate([thp[0], thn[0], thp[1], thn[1], thp[2], thn[2]]):
                    ve(lambda eng, j_=j_, c=col: eng.tensor_copy(tmpA[:, j_:j_ + 1], c[:]))
                return finish_debug([tmpA[:]] + tiles[:7] +
                                    [sig_bf[j_][:] for j_ in range(4)])
            prev_v = 0
            for i in range(DC):
                wd = dma(wstC[i % 2][:], W2T[i * 128:(i + 1) * 128, c0:c0 + 512],
                         waits=(stC[i % 2],))
                ve(lambda eng, i=i: eng.tensor_scalar(
                    sgtmp[0][:], wstC[i % 2][:], thp[1][:, 0:1], None, op0=ALU.is_gt),
                   waits=(("ds", wd),))
                prev_v = ve(lambda eng, i=i: eng.tensor_scalar(
                    sgtmp[1][:], wstC[i % 2][:], thn[1][:, 0:1], None, op0=ALU.is_lt))
                stC[i % 2] = ("dve", prev_v)
                wq_done = ve(lambda eng, i=i: eng.tensor_tensor(
                    wblk[i][:], sgtmp[0][:], sgtmp[1][:], op=ALU.subtract),
                    waits=(("pe", wblk_rd[i]),))
                for j in range(4):
                    for hf in range(2):
                        wx2 = pe(lambda eng, i=i, j=j, hf=hf: eng.matmul(
                            PS4[j][:, hf * 512:(hf + 1) * 512],
                            wblk[i][:, j * 128:(j + 1) * 128],
                            r2b[i][:, hf * 512:(hf + 1) * 512],
                            start=(i == 0), stop=(i == DC - 1)),
                            waits=(("dve", wq_done),
                                   ("act", wsg[j])),
                            inc=(i == DC - 1 and j == 3 and hf == 1))
                wblk_rd[i] = p.n("pe")
            for j in range(4):
                hc = hb * 4 + j
                sig_used[j] = ve(lambda eng, j=j, hc=hc: eng.tensor_tensor(
                    hidden[hc][:], sig_bf[j][:], PS4[j][:], op=ALU.mult),
                    waits=(("pe", wx2),))
                x2_used[j] = sig_used[j]
        p.barrier()

        if trunc == 7:
            return finish_debug([hidden[i][:] for i in range(16)])
        r3_w = 0
        wq3_prev = 0
        w3q_rd = [0] * 22
        st3 = [0, 0]
        for g4 in range(4):
            d0_ = g4 * 512
            for half in range(2):
                prev_sg = 0
                for hcl in range(22):
                    hc = half * 22 + hcl
                    wd = dma(w3st[hcl % 2][:], W3T[hc * 128:(hc + 1) * 128,
                                                   d0_:d0_ + 512],
                             waits=(("act", st3[hcl % 2]),))
                    sc(lambda eng, hcl=hcl: eng.sign(sgtmp[0][:], w3st[hcl % 2][:],
                                                     bias=thn[2][:, 0:1]),
                       waits=(("ds", wd), ("dve", wq3_prev)))
                    prev_sg = sc(lambda eng, hcl=hcl: eng.sign(
                        sgtmp[1][:], w3st[hcl % 2][:], bias=thp[2][:, 0:1]))
                    st3[hcl % 2] = prev_sg
                    wq3 = ve(lambda eng, hcl=hcl: eng.tensor_tensor(
                        w3q[hcl][:], sgtmp[0][:], sgtmp[1][:], op=ALU.add),
                        waits=(("act", prev_sg), ("pe", w3q_rd[hcl])))
                    wq3_prev = wq3
                    st_ = (half == 0 and hcl == 0)
                    sp_ = (half == 1 and hcl == 21)
                    for j in range(4):
                        for hf in range(2):
                            wo3 = pe(lambda eng, j=j, hf=hf, hcl=hcl, hc=hc,
                                     st_=st_, sp_=sp_: eng.matmul(
                                PS4[j][:, hf * 512:(hf + 1) * 512],
                                w3q[hcl][:, j * 128:(j + 1) * 128],
                                hidden[hc][:, hf * 512:(hf + 1) * 512],
                                start=st_, stop=sp_),
                                waits=(("dve", max(wq3, r3_w)),),
                                inc=(sp_ and j == 3 and hf == 1))
                    w3q_rd[hcl] = p.n("pe")
            for j in range(4):
                do = g4 * 4 + j
                r3_w = ve(lambda eng, do=do, j=j: eng.scalar_tensor_tensor(
                    r2b[do][:], r2b[do][:], 2.0, PS4[j][:],
                    op0=ALU.mult, op1=ALU.add),
                    waits=(("pe", wo3),))
        p.barrier()

        y_w = [0, 0]

        def ln2_writer(i):
            w = ve(lambda eng, i=i: eng.tensor_tensor(
                y_sb[i % 2][:], t_r1[:], PB[:], op=ALU.add),
                waits=((("out", 16 * (i - 1)) if i >= 2 else ("out", 0)),))
            p.count["outn"] = p.count.get("outn", 0) + 1
            p.emit("sync", lambda eng, i=i: eng.dma_start(
                Y[i * 128:(i + 1) * 128, :], y_sb[i % 2][:]),
                waits=(("dve", w),), inc=("out", 16))
            return w

        layer_norm(r2b, False, ln2_writer)
        p.wait("sync", "out", 16 * DC)
        p.barrier()

        import os as _os
        if _os.environ.get("ATH_DUMP_PLAN"):
            for e in ENGINES:
                print(f"== {e} ({len(p.stream[e])}) ==")
                for op in p.stream[e]:
                    if op[0] == "w":
                        print(f"  wait {op[1]} >= {op[2]}")
                    else:
                        import inspect as _in
                        try:
                            src = _in.getsource(op[1]).strip().split("eng.")[1][:90]
                        except Exception:
                            src = "?"
                        print(f"  x {src}  inc={op[2]}")
        with nc.Block() as block:
            def runner(name):
                def body(eng):
                    for op in p.stream[name]:
                        if op[0] == "w":
                            eng.wait_ge(sems[op[1]], op[2])
                        else:
                            inst = op[1](eng)
                            if op[2] is not None:
                                inst.then_inc(sems[op[2][0]], op[2][1])
                return body
            block.sync(runner("sync"))
            block.gpsimd(runner("gpsimd"))
            block.tensor(runner("tensor"))
            block.scalar(runner("scalar"))
            block.vector(runner("vector"))
    return nc


_CACHE = {}


def _get_program():
    if "nc" not in _CACHE:
        _CACHE["nc"] = build_program()
    return _CACHE["nc"]


def kernel(L0_attn_x, L0_attn_Wq, L0_attn_bq, L0_attn_Wk, L0_attn_bk,
           L0_attn_Wv, L0_attn_bv, L0_attn_Wo, L0_attn_bo,
           L0_ffn_w1, L0_ffn_w2, L0_ffn_w3):
    x = np.asarray(L0_attn_x, dtype=np.float32)
    f32 = np.float32
    WQ = np.ascontiguousarray(L0_attn_Wq, dtype=f32)
    WK = np.ascontiguousarray(L0_attn_Wk, dtype=f32)
    WV = np.ascontiguousarray(L0_attn_Wv, dtype=f32)
    WO = np.ascontiguousarray(L0_attn_Wo, dtype=f32)
    BQ = np.ascontiguousarray(np.asarray(L0_attn_bq, f32).reshape(NH, 128).T)
    BO = np.ascontiguousarray(np.asarray(L0_attn_bo, f32).reshape(NH, 128).T)
    BK = np.ascontiguousarray(np.asarray(L0_attn_bk, f32).reshape(128, 1))
    BVR = np.ascontiguousarray(np.asarray(L0_attn_bv, f32).reshape(1, 128))
    W1T = np.ascontiguousarray(np.asarray(L0_ffn_w1, f32).T)
    W2T = np.ascontiguousarray(np.asarray(L0_ffn_w2, f32).T)
    W3T = np.ascontiguousarray(np.asarray(L0_ffn_w3, f32).T)
    POS = np.ascontiguousarray(
        np.broadcast_to(np.arange(T, dtype=f32)[None, :], (128, T)))
    IROW = (-(np.arange(128) % 64).astype(f32)[:, None] / f32(64.0)).astype(f32)
    SGNC = np.where(np.arange(128) < 64, f32(-1.0), f32(1.0))[:, None].astype(f32)
    PSWP = np.zeros((128, 128), f32)
    PSWP[(np.arange(128) + 64) % 128, np.arange(128)] = 1.0

    nc = _get_program()
    in_maps = []
    xT = [np.ascontiguousarray(x[b].T) for b in range(4)]
    for c in range(8):
        b, h = c // 2, c % 2
        POSQ = np.ascontiguousarray(
            np.broadcast_to(np.arange(h * Q, (h + 1) * Q, dtype=f32)[None, :],
                            (128, Q)))
        in_maps.append({
            "XF": xT[b],
            "XO": np.ascontiguousarray(xT[b][:, h * Q:(h + 1) * Q]),
            "POS": POS, "POSQ": POSQ, "IROW": IROW, "SGNC": SGNC, "PSWP": PSWP,
            "WQ": WQ, "WK": WK, "WV": WV, "WO": WO,
            "BQ": BQ, "BO": BO, "BK": BK, "BVR": BVR,
            "W1T": W1T, "W2T": W2T, "W3T": W3T,
        })
    res = run_bass_kernel_spmd(nc, in_maps, core_ids=list(range(8)))
    out = np.empty((4, T, D), np.float32)
    for c in range(8):
        b, h = c // 2, c % 2
        out[b, h * Q:(h + 1) * Q, :] = np.asarray(res.results[c]["Y"]).T
    return out
